# version 01
# baseline (speedup 1.0000x reference)
import contextlib
import numpy as np
import ml_dtypes
import concourse.bass as bass
import concourse.mybir as mybir
from concourse.bass_utils import run_bass_kernel_spmd

F32 = mybir.dt.float32
BF16 = mybir.dt.bfloat16
I16 = mybir.dt.int16
AF = mybir.ActivationFunctionType
ALU = mybir.AluOpType
AX = mybir.AxisListType

NCORES = 8
P = 128
S = 2048
BC = 2
T = S * BC
D = 1024
NCH = D // P
HEADS = 16
HD = 64
LM = 128
SEG = 16
NMAT = BC * HEADS
S4 = 1.0 / float(np.sqrt(np.sqrt(HD)))
NT = T // P
I_FFN = 2816
H_FFN = 4096
NTB = T // 512
BF = ml_dtypes.bfloat16


class Chain:

    def __init__(self, eng, sem):
        self.eng, self.sem, self.n = eng, sem, 0

    def bar(self, inst):
        self.n += 1
        inst.then_inc(self.sem, 1)
        self.eng.wait_ge(self.sem, self.n)
        return inst



def build_AB(stages=3, cuts=(), nt=NT):
    nc = bass.Bass("TRN2", target_bir_lowering=False)
    x_in = nc.dram_tensor("x", [T, D], F32, kind="ExternalInput").ap()
    wqT_in = nc.dram_tensor("wqT", [D, D], F32, kind="ExternalInput").ap()
    wkT_in = nc.dram_tensor("wkT", [D, D], F32, kind="ExternalInput").ap()
    wvT_in = nc.dram_tensor("wvT", [D, D], F32, kind="ExternalInput").ap()
    bqc_in = nc.dram_tensor("bqc", [P, NCH], F32, kind="ExternalInput").ap()
    bkc_in = nc.dram_tensor("bkc", [P, NCH], F32, kind="ExternalInput").ap()
    bvr_in = nc.dram_tensor("bvr", [1, D], F32, kind="ExternalInput").ap()
    eps_in = nc.dram_tensor("epsc", [P, 1], F32, kind="ExternalInput").ap()

    xn_tok_o = nc.dram_tensor("xn_tok", [T, D], F32, kind="ExternalOutput").ap()
    q_fm_o = nc.dram_tensor("q_fm", [NCH * P, T], BF16, kind="ExternalOutput").ap()
    k_fm_o = nc.dram_tensor("k_fm", [NCH * P, T], BF16, kind="ExternalOutput").ap()
    v_tok_o = nc.dram_tensor("v_tok", [T, D], BF16, kind="ExternalOutput").ap()
    xn_fm_o = nc.dram_tensor("xn_fmo", [NCH * P, T], BF16, kind="ExternalOutput").ap()
    qlm_o = nc.dram_tensor("qlm", [NCH * P, BC * LM], BF16, kind="ExternalOutput").ap()
    klm_o = nc.dram_tensor("klm", [NCH * P, BC * LM], BF16, kind="ExternalOutput").ap()
    k2_o = nc.dram_tensor("k2", [NMAT * P, LM], BF16, kind="ExternalOutput").ap()
    pcmax_o = nc.dram_tensor("pcmax", [1, 4], F32, kind="ExternalOutput").ap()

    es = contextlib.ExitStack()
    sbt = lambda name, shape, dt=F32: es.enter_context(nc.sbuf_tensor(name, shape, dt))
    pst = lambda name, shape, dt=F32: es.enter_context(nc.psum_tensor(name, shape, dt))
    sem = lambda name: es.enter_context(nc.semaphore(name))

    NXB = 2
    x_t = [sbt(f"x_t{i}", [P, D]) for i in range(NXB)]
    xn_t = [sbt(f"xn_t{i}", [P, D]) for i in range(NXB)]
    xnb_t = [sbt(f"xnb_t{i}", [P, D], BF16) for i in range(2)]
    stat = sbt("stat", [P, 8])
    epsc = sbt("epsc_s", [P, 1])
    xn_fm = [es.enter_context(nc.sbuf_tensor(f"xn_fm{c}", [P, T], BF16, is_transpose_dest=True)) for c in range(NCH)]
    wq = [sbt(f"wqc{c}", [P, D], BF16) for c in range(NCH)]
    wk = [sbt(f"wkc{c}", [P, D], BF16) for c in range(NCH)]
    wv = [sbt(f"wvc{c}", [P, D], BF16) for c in range(NCH)]
    bqs = sbt("bqs", [P, NCH])
    bks = sbt("bks", [P, NCH])
    bvf = sbt("bvf", [1, D])
    bvb = sbt("bvb", [1, D], BF16)
    ones_row = sbt("ones_row", [1, P], BF16)
    ones_col = sbt("ones_col", [P, 1])
    if stages >= 2:
        pbf_s = sbt("pbf_s", [P, T], BF16)
        pbf = [pbf_s, pbf_s]
        lmraw = [sbt(f"lmraw{i}", [P, BC * LM]) for i in range(2)]
        b16q = sbt("b16q", [P, NCH])
        b16k = sbt("b16k", [P, NCH])
        _wldv = pbf_s[:].bitcast(F32)
        wld = [_wldv[:, 0:D], _wldv[:, D:2 * D]]
    else:
        wld_t = [sbt(f"wld{i}", [P, D]) for i in range(2)]
        wld = [wld_t[0][:], wld_t[1][:]]
        xsq_t = sbt("xsq1", [P, D])
        xsq = xsq_t[:]
    if stages >= 2:
        vf32_s = sbt("vf32_s", [P, D])
        vbf_s = sbt("vbf_s", [P, D], BF16)
        vf32 = [vf32_s, vf32_s]
        vbf = [vbf_s, vbf_s]
        qlm_b = sbt("qlm_b", [P, NCH * BC * LM], BF16)
        klm_b = sbt("klm_b", [P, NCH * BC * LM], BF16)
        xsq = qlm_b[:].bitcast(F32)[:, 0:D]
        e2 = sbt("e2", [P, LM])
        k2f = [sbt(f"k2f{i}", [P, LM]) for i in range(2)]
        k2bs = [sbt(f"k2bs{i}", [P, LM], BF16) for i in range(2)]
        s2red = sbt("s2red", [P, 4])
    runmax = sbt("runmax", [1, LM])
    pcm = sbt("pcm", [1, 4])

    ps_q = [pst(f"ps_q{i}", [P, 512]) for i in range(4)]
    ps_s2 = [pst(f"ps_s2{i}", [P, LM]) for i in range(2)]
    ps_cs = [pst(f"ps_cs{i}", [1, LM]) for i in range(2)]

    blk = es.enter_context(nc.Block())
    s_ldw = sem("s_ldw")
    s_wbf = sem("s_wbf")
    s_x = sem("s_x")
    s_v = sem("s_v")
    s_a = sem("s_a")
    s_tr = sem("s_tr")
    s_st = sem("s_st")
    s_pe = sem("s_pe")
    s_ev = sem("s_ev")
    s_lm = sem("s_lm")
    s_ps = sem("s_ps")
    s_v2 = sem("s_v2")
    s_s2a = sem("s_s2a")
    s_fin = sem("s_fin")
    s_vch = sem("s_vch")
    s_c = sem("s_c")
    s_lr = sem("s_lr")
    s_lw0 = sem("s_lw0")
    s_lw1 = sem("s_lw1")
    s_ach = sem("s_ach")
    s_ks = sem("s_ks")

    NQK = 2 * NCH * NTB
    NVG = NT * 2
    NG = NQK + NVG

    @blk.sync
    def _(sync):
        for wi in range(0 if "noW" in cuts else 3):
            w_in = (wqT_in, wkT_in, wvT_in)[wi]
            for c in range(NCH):
                k = wi * NCH + c
                if k >= 2:
                    sync.wait_ge(s_wbf, k - 1)
                sync.dma_start(out=wld[k % 2], in_=w_in[c * P:(c + 1) * P, :]).then_inc(
                    (s_lw0, s_lw1)[k % 2], 16)
        nw = 0 if "noW" in cuts else 24
        if "noW" not in cuts:
            sync.dma_start(out=bqs[:], in_=bqc_in).then_inc(s_ldw, 16)
            sync.dma_start(out=bks[:], in_=bkc_in).then_inc(s_ldw, 16)
            sync.dma_start(out=bvf[:], in_=bvr_in).then_inc(s_ldw, 16)
        sync.dma_start(out=epsc[:], in_=eps_in).then_inc(s_ldw, 16)
        for i in range(nt):
            if i >= NXB:
                sync.wait_ge(s_a, 2 * (i - NXB) + 2)
                sync.dma_start(out=xn_tok_o[(i - NXB) * P:(i - NXB + 1) * P, :],
                               in_=xn_t[(i - NXB) % NXB][:]).then_inc(s_st, 16)
            sync.dma_start(out=x_t[i % NXB][:], in_=x_in[i * P:(i + 1) * P, :]).then_inc(s_x, 16)
        for i in range(nt - NXB, nt):
            sync.wait_ge(s_a, 2 * i + 2)
            sync.dma_start(out=xn_tok_o[i * P:(i + 1) * P, :],
                           in_=xn_t[i % NXB][:]).then_inc(s_st, 16)
        for g in range(16 if stages >= 2 else 0):
            proj, oc = divmod(g, NCH)
            sync.wait_ge(s_ev, 8 * (g + 1))
            dst = (q_fm_o, k_fm_o)[proj]
            sync.dma_start(out=dst[oc * P:(oc + 1) * P, :], in_=pbf[g % 2][:]).then_inc(s_ps, 16)
        for tt in range(0 if ("noV" in cuts or stages < 2) else NT):
            sync.wait_ge(s_ev, NQK + 2 * tt + 2)
            sync.dma_start(out=v_tok_o[tt * P:(tt + 1) * P, :], in_=vbf[tt % 2][:]).then_inc(s_ps, 16)
        sync.wait_ge(s_tr, nt * NCH * 16)
        for c in range(NCH if "noFM" not in cuts else 0):
            sync.dma_start(out=xn_fm_o[c * P:(c + 1) * P, :], in_=xn_fm[c][:]).then_inc(s_fin, 16)
        if stages >= 2:
            sync.wait_ge(s_lm, 16)
        if stages >= 2 and "noLM" not in cuts:
            sync.dma_start(out=qlm_o.rearrange("(c p) l -> p c l", p=P), in_=qlm_b[:].rearrange("p (c l) -> p c l", c=NCH)).then_inc(s_fin, 16)
            sync.dma_start(out=klm_o.rearrange("(c p) l -> p c l", p=P), in_=klm_b[:].rearrange("p (c l) -> p c l", c=NCH)).then_inc(s_fin, 16)
        for m in range(NMAT if stages >= 3 else 0):
            sync.wait_ge(s_v2, 3 * m + 2)
            sync.dma_start(out=k2_o[m * P:(m + 1) * P, :], in_=k2bs[m % 2][:]).then_inc(s_ks, 16)
        if stages >= 3:
            sync.wait_ge(s_v2, 3 * NMAT + 1)
            sync.wait_ge(s_ks, NMAT * 16)
        sync.wait_ge(s_c, 1)
        sync.dma_start(out=pcmax_o, in_=pcm[:]).then_inc(s_fin, 16)
        nfin = ((0 if "noFM" in cuts else NCH) + 1 + (2 if (stages >= 2 and "noLM" not in cuts) else 0)) * 16
        sync.wait_ge(s_fin, nfin)
        if stages >= 2:
            sync.wait_ge(s_ps, (16 if "noV" in cuts else 48) * 16)
        sync.wait_ge(s_st, nt * 16)

    @blk.gpsimd
    def _(gpsimd):
        gpsimd.memset(ones_row[:], 1.0)
        gpsimd.memset(ones_col[:], 1.0)
        gpsimd.memset(runmax[:], -3.0e38)
        gpsimd.memset(pcm[:], 0.0).then_inc(s_c, 1)

    @blk.vector
    def _(vector):
        vch = Chain(vector, s_vch)
        for wi in range(0 if "noW" in cuts else 3):
            wtiles = (wq, wk, wv)[wi]
            for c in range(NCH):
                k = wi * NCH + c
                vector.wait_ge((s_lw0, s_lw1)[k % 2], (k // 2 + 1) * 16)
                vector.tensor_copy(out=wtiles[c][:], in_=wld[k % 2]).then_inc(s_wbf, 1)
        if "noW" not in cuts:
            vector.wait_ge(s_ldw, 4 * 16)
            vch.bar(vector.tensor_scalar_mul(out=bqs[:], in0=bqs[:], scalar1=S4))
            vch.bar(vector.tensor_scalar_mul(out=bks[:], in0=bks[:], scalar1=S4))
            if stages >= 2:
                vch.bar(vector.tensor_scalar_mul(out=b16q[:], in0=bqs[:], scalar1=16.0))
                vch.bar(vector.tensor_scalar_mul(out=b16k[:], in0=bks[:], scalar1=16.0))
            vector.tensor_copy(out=bvb[:], in_=bvf[:]).then_inc(s_v, 1)
        else:
            vector.wait_ge(s_ldw, 16)
            vector.sem_inc(s_v, 1)
        for i in range(NT):
            b = i % NXB
            vector.wait_ge(s_x, (i + 1) * 16)
            vch.bar(vector.tensor_reduce(out=stat[:, 0:1], in_=x_t[b][:], axis=AX.X, op=ALU.add))
            vch.bar(vector.tensor_tensor(out=xsq, in0=x_t[b][:], in1=x_t[b][:], op=ALU.mult))
            vch.bar(vector.tensor_reduce(out=stat[:, 1:2], in_=xsq, axis=AX.X, op=ALU.add))
            vch.bar(vector.tensor_scalar_mul(out=stat[:, 2:3], in0=stat[:, 0:1], scalar1=1.0 / D))
            vch.bar(vector.tensor_scalar_mul(out=stat[:, 3:4], in0=stat[:, 1:2], scalar1=1.0 / D))
            vch.bar(vector.tensor_tensor(out=stat[:, 4:5], in0=stat[:, 2:3], in1=stat[:, 2:3], op=ALU.mult))
            vector.tensor_tensor(out=stat[:, 4:5], in0=stat[:, 3:4], in1=stat[:, 4:5],
                                 op=ALU.subtract).then_inc(s_v, 1)
            vector.wait_ge(s_a, 2 * i + 1)
            vector.scalar_tensor_tensor(out=stat[:, 6:7], in0=stat[:, 2:3], scalar=-1.0,
                                        in1=stat[:, 5:6], op0=ALU.mult,
                                        op1=ALU.mult).then_inc(s_v, 1)
            vector.wait_ge(s_a, 2 * i + 2)
            if i >= 2:
                vector.wait_ge(s_tr, (i - 1) * NCH * 16)
            vector.tensor_copy(out=xnb_t[i % 2][:], in_=xn_t[b][:]).then_inc(s_v, 1)
        for k in range(NQK if stages >= 2 else 0):
            g, tb = divmod(k, NTB)
            proj, oc = divmod(g, NCH)
            vector.wait_ge(s_pe, k + 1)
            if "noLM" in cuts:
                vector.sem_inc(s_lr, 1)
            else:
                vch.bar(vector.tensor_reduce(out=lmraw[g % 2][:, tb * 32:(tb + 1) * 32],
                                     in_=ps_q[k % 4][:].rearrange("p (n w) -> p n w", w=SEG),
                                     axis=AX.X, op=ALU.add))
                vector.sem_inc(s_lr, 1)
            if tb == NTB - 1:
                if "noLM" in cuts:
                    vector.sem_inc(s_lm, 1)
                else:
                    dstlm = (qlm_b, klm_b)[proj]
                    b16 = (b16q, b16k)[proj]
                    vector.tensor_scalar(out=dstlm[:, oc * BC * LM:(oc + 1) * BC * LM],
                                         in0=lmraw[g % 2][:], scalar1=S4,
                                         scalar2=b16[:, oc:oc + 1], op0=ALU.mult,
                                         op1=ALU.add).then_inc(s_lm, 1)
        vector.wait_ge(s_c, 1)
        for m in range(NMAT if stages >= 3 else 0):
            vector.wait_ge(s_pe, NG + 2 * m + 1)
            vch.bar(vector.tensor_reduce(out=s2red[:, 0:1], in_=ps_s2[m % 2][:], axis=AX.X, op=ALU.max))
            vector.tensor_scalar_mul(out=s2red[:, 1:2], in0=s2red[:, 0:1],
                                     scalar1=-1.0 / 256.0).then_inc(s_v2, 1)
            vector.wait_ge(s_s2a, m + 1)
            if m >= 2:
                vector.wait_ge(s_pe, NG + 2 * (m - 2) + 2)
            vch.bar(vector.tensor_reduce(out=s2red[:, 2:3], in_=e2[:], axis=AX.X, op=ALU.add))
            vch.bar(vector.reciprocal(out=s2red[:, 3:4], in_=s2red[:, 2:3]))
            vch.bar(vector.tensor_scalar_mul(out=k2f[m % 2][:], in0=e2[:], scalar1=s2red[:, 3:4]))
            if m >= 2:
                vector.wait_ge(s_ks, 16 * (m - 1))
            vector.tensor_copy(out=k2bs[m % 2][:],
                               in_=k2f[m % 2][:]).then_inc(s_v2, 1)
            vector.wait_ge(s_pe, NG + 2 * m + 2)
            vector.tensor_tensor(out=runmax[:], in0=runmax[:], in1=ps_cs[m % 2][0:1, :],
                                 op=ALU.max).then_inc(s_v2, 1)
        if stages >= 3:
            vector.wait_ge(s_v2, 3 * NMAT)
            vector.tensor_reduce(out=pcm[0:1, 0:1], in_=runmax[0:1, :], axis=AX.X,
                                 op=ALU.max).then_inc(s_v2, 1)

    @blk.scalar
    def _(scalar):
        ach = Chain(scalar, s_ach)
        for i in range(nt):
            b = i % NXB
            scalar.wait_ge(s_v, 1 + 3 * i + 1)
            ach.bar(scalar.activation(out=stat[:, 5:6], in_=stat[:, 4:5], func=AF.Ln, bias=epsc[:, 0:1]))
            scalar.activation(out=stat[:, 5:6], in_=stat[:, 5:6], func=AF.Exp,
                              scale=-0.5).then_inc(s_a, 1)
            scalar.wait_ge(s_v, 1 + 3 * i + 2)
            if i >= NXB:
                scalar.wait_ge(s_st, (i - NXB + 1) * 16)
            scalar.activation(out=xn_t[b][:], in_=x_t[b][:], func=AF.Identity,
                              scale=stat[:, 5:6], bias=stat[:, 6:7]).then_inc(s_a, 1)
            scalar.wait_ge(s_v, 1 + 3 * i + 3)
            for c in range(NCH):
                if "noTR" in cuts:
                    scalar.sem_inc(s_tr, 16)
                else:
                    scalar.dma_start(out=xn_fm[c][:, i * P:(i + 1) * P],
                                     in_=xnb_t[i % 2][:, c * P:(c + 1) * P],
                                     transpose=True).then_inc(s_tr, 16)
        for k in range(NQK if stages >= 2 else 0):
            g, tb = divmod(k, NTB)
            proj, oc = divmod(g, NCH)
            scalar.wait_ge(s_lr, k + 1)
            if tb == 0 and g >= 1:
                scalar.wait_ge(s_ps, 16 * g)
            bias = (bqs, bks)[proj]
            scalar.activation(out=pbf[g % 2][:, tb * 512:(tb + 1) * 512],
                              in_=ps_q[k % 4][:], func=AF.Identity, scale=S4,
                              bias=bias[:, oc:oc + 1]).then_inc(s_ev, 1)
        for kv in range(0 if ("noV" in cuts or stages < 2) else NVG):
            tt, oh = divmod(kv, 2)
            k = NQK + kv
            scalar.wait_ge(s_pe, k + 1)
            if oh == 0 and tt >= 1:
                scalar.wait_ge(s_ps, 16 * (16 + tt))
            scalar.activation(out=vbf[tt % 2][:, oh * 512:(oh + 1) * 512],
                              in_=ps_q[k % 4][:], func=AF.Copy).then_inc(s_ev, 1)
        for m in range(NMAT if stages >= 3 else 0):
            scalar.wait_ge(s_v2, 3 * m + 1)
            scalar.activation(out=e2[:], in_=ps_s2[m % 2][:], func=AF.Exp,
                              scale=1.0 / 256.0, bias=s2red[:, 1:2]).then_inc(s_s2a, 1)

    @blk.tensor
    def _(tensor):
        if stages < 2:
            return
        tensor.wait_ge(s_tr, NT * NCH * 16)
        tensor.wait_ge(s_wbf, 24)
        tensor.wait_ge(s_v, 1)
        tensor.wait_ge(s_c, 1)
        for k in range(NQK):
            g, tb = divmod(k, NTB)
            proj, oc = divmod(g, NCH)
            w = (wq, wk)[proj]
            if k >= 4:
                tensor.wait_ge(s_ev, k - 3)
            mi = None
            for cc in range(NCH):
                mi = tensor.matmul(ps_q[k % 4][:], w[cc][:, oc * P:(oc + 1) * P],
                                   xn_fm[cc][:, tb * 512:(tb + 1) * 512],
                                   start=(cc == 0), stop=(cc == NCH - 1))
            mi.then_inc(s_pe, 1)
        for kv in range(0 if "noV" in cuts else NVG):
            tt, oh = divmod(kv, 2)
            k = NQK + kv
            tensor.wait_ge(s_ev, k - 3)
            for cc in range(NCH):
                tensor.matmul(ps_q[k % 4][:], xn_fm[cc][:, tt * P:(tt + 1) * P],
                              wv[cc][:, oh * 512:(oh + 1) * 512],
                              start=(cc == 0), stop=False)
            tensor.matmul(ps_q[k % 4][:], ones_row[0:1, :], bvb[0:1, oh * 512:(oh + 1) * 512],
                          start=False, stop=True).then_inc(s_pe, 1)
        if stages < 3:
            return
        tensor.wait_ge(s_lm, 16)
        for m in range(NMAT):
            sq, h = divmod(m, HEADS)
            cc, half = divmod(h, 2)
            prow = 64 * half
            if m >= 2:
                tensor.wait_ge(s_s2a, m - 1)
            qs = qlm_b[prow:prow + 64, cc * BC * LM + sq * LM: cc * BC * LM + (sq + 1) * LM]
            ks = klm_b[prow:prow + 64, cc * BC * LM + sq * LM: cc * BC * LM + (sq + 1) * LM]
            tensor.matmul(ps_s2[m % 2][:], qs, ks, start=True, stop=True).then_inc(s_pe, 1)
            tensor.wait_ge(s_v2, 3 * m + 2)
            tensor.matmul(ps_cs[m % 2][:], ones_col[:], k2f[m % 2][:],
                          start=True, stop=True).then_inc(s_pe, 1)

    import os
    if os.environ.get("AB_DEBUG_SBUF"):
        print(f"AB stages={stages} sbuf base={nc.sbuf_base} top={nc.sbuf_top} remaining={nc.sbuf_bytes_remaining}")
    es.close()
    return nc


AB_INPUTS = ["x", "wqT", "wkT", "wvT", "bqc", "bkc", "bvr", "epsc"]


def ab_in_maps(inputs, core):
    x = np.asarray(inputs["L0_attn_x"])
    xs = x[core * BC:(core + 1) * BC].reshape(T, D)
    return {
        "x": np.ascontiguousarray(xs),
        "wqT": np.ascontiguousarray(np.asarray(inputs["L0_attn_Wq"]).T),
        "wkT": np.ascontiguousarray(np.asarray(inputs["L0_attn_Wk"]).T),
        "wvT": np.ascontiguousarray(np.asarray(inputs["L0_attn_Wv"]).T),
        "bqc": np.ascontiguousarray(np.asarray(inputs["L0_attn_bq"]).reshape(NCH, P).T),
        "bkc": np.ascontiguousarray(np.asarray(inputs["L0_attn_bk"]).reshape(NCH, P).T),
        "bvr": np.ascontiguousarray(np.asarray(inputs["L0_attn_bv"]).reshape(1, D)),
        "epsc": np.full((P, 1), 1e-5, np.float32),
    }





class SP:
    def __init__(self, nc):
        self.nc = nc
        self.ops = []

    def add(self, eng, fn, dma=False):
        self.ops.append((eng, fn, 16 if dma else 1))

    def emit(self, es):
        nc = self.nc
        sq = es.enter_context(nc.semaphore("s_q"))
        pre = []
        c = 0
        for (_, _, inc) in self.ops:
            pre.append(c)
            c += inc
        total = c
        blk = es.enter_context(nc.Block())
        mine = {}
        for i, (e, f, inc) in enumerate(self.ops):
            mine.setdefault(e, []).append(i)

        def mk(ename):
            idxs = mine.get(ename, [])

            def body(eng):
                for i in idxs:
                    if pre[i] > 0:
                        eng.wait_ge(sq, pre[i])
                    inst = self.ops[i][1](eng)
                    inst.then_inc(sq, self.ops[i][2])
                if ename == "sync":
                    eng.wait_ge(sq, total)
            return body

        blk.sync(mk("sync"))
        blk.vector(mk("vector"))
        blk.scalar(mk("scalar"))
        blk.tensor(mk("tensor"))
        blk.gpsimd(mk("gpsimd"))
        return total



def build_C():
    nc = bass.Bass("TRN2", target_bir_lowering=False)
    k2_in = nc.dram_tensor("k2", [NMAT * P, LM], BF16, kind="ExternalInput").ap()
    amax_in = nc.dram_tensor("amax", [1, 8], F32, kind="ExternalInput").ap()
    eyes_in = nc.dram_tensor("eyes", [P, 3 * P], F32, kind="ExternalInput").ap()
    identb_in = nc.dram_tensor("identb", [P, P], BF16, kind="ExternalInput").ap()
    z_o = nc.dram_tensor("z", [NMAT * P, LM], BF16, kind="ExternalOutput").ap()
    zt_o = nc.dram_tensor("zt", [NMAT * P, LM], BF16, kind="ExternalOutput").ap()

    es = contextlib.ExitStack()
    sbt = lambda n, s, dt=F32: es.enter_context(nc.sbuf_tensor(n, s, dt))
    pst = lambda n, s, dt=F32: es.enter_context(nc.psum_tensor(n, s, dt))
    k2m = sbt("k2m", [P, LM], BF16)
    amax = sbt("amax_s", [1, 8])
    gm = sbt("gm", [1, 1])
    rc = sbt("rc", [1, 1])
    rcb = sbt("rcb", [P, 1])
    onesr = sbt("onesr", [1, P])
    eyes = sbt("eyes_s", [P, 3 * P])
    eyesb = sbt("eyesb", [P, 3 * P], BF16)
    identb = sbt("identb_s", [P, P], BF16)
    K2T = sbt("K2T", [P, LM], BF16)
    Vt = sbt("Vt", [P, LM], BF16)
    VTt = sbt("VTt", [P, LM], BF16)
    ATs = sbt("ATs", [P, LM], BF16)
    N2 = sbt("N2", [P, LM], BF16)
    N1 = sbt("N1", [P, LM], BF16)
    Mt = sbt("Mt", [P, LM], BF16)
    ps_a = pst("ps_a", [P, LM])
    ps_b = pst("ps_b", [P, LM])
    ps_t = pst("ps_t", [P, LM], BF16)
    ps_r = pst("ps_r", [P, 1])

    sp = SP(nc)
    A = sp.add
    A("sync", lambda e: e.dma_start(out=amax[:], in_=amax_in), dma=True)
    A("sync", lambda e: e.dma_start(out=eyes[:], in_=eyes_in), dma=True)
    A("sync", lambda e: e.dma_start(out=identb[:], in_=identb_in), dma=True)
    A("gpsimd", lambda e: e.memset(onesr[:], 1.0))
    A("vector", lambda e: e.tensor_reduce(out=gm[:], in_=amax[:], axis=AX.X, op=ALU.max))
    A("vector", lambda e: e.reciprocal(out=rc[:], in_=gm[:]))
    A("vector", lambda e: e.tensor_copy(out=eyesb[:], in_=eyes[:]))
    A("tensor", lambda e: e.matmul(ps_r[:], onesr[0:1, :], rc[0:1, :], start=True, stop=True))
    A("scalar", lambda e: e.activation(out=rcb[:], in_=ps_r[:], func=AF.Copy))
    for m in range(NMAT):
        mm = m
        A("sync", lambda e, m=mm: e.dma_start(out=k2m[:], in_=k2_in[m * P:(m + 1) * P, :]), dma=True)
        A("tensor", lambda e: e.transpose(ps_t[:], k2m[:], identb[:]))
        A("scalar", lambda e: e.activation(out=K2T[:], in_=ps_t[:], func=AF.Copy))
        A("scalar", lambda e: e.activation(out=Vt[:], in_=ps_t[:], func=AF.Identity, scale=rcb[:, 0:1]))
        A("vector", lambda e: e.tensor_scalar(out=VTt[:], in0=k2m[:], scalar1=rcb[:, 0:1],
                                              scalar2=None, op0=ALU.mult))
        for it in range(6):
            A("tensor", lambda e: e.matmul(ps_a[:], K2T[:], Vt[:], start=True, stop=True))
            A("tensor", lambda e: e.matmul(ps_b[:], Vt[:], K2T[:], start=True, stop=True))
            A("vector", lambda e: e.scalar_tensor_tensor(out=N2[:], in0=ps_a[:], scalar=-1.0,
                                                         in1=eyesb[:, 0:P], op0=ALU.mult, op1=ALU.add))
            A("scalar", lambda e: e.activation(out=ATs[:], in_=ps_b[:], func=AF.Copy))
            A("tensor", lambda e: e.matmul(ps_a[:], ATs[:], N2[:], start=True, stop=True))
            A("vector", lambda e: e.scalar_tensor_tensor(out=N1[:], in0=ps_a[:], scalar=-1.0,
                                                         in1=eyesb[:, P:2 * P], op0=ALU.mult, op1=ALU.add))
            A("tensor", lambda e: e.matmul(ps_a[:], ATs[:], N1[:], start=True, stop=True))
            A("vector", lambda e: e.scalar_tensor_tensor(out=Mt[:], in0=ps_a[:], scalar=-1.0,
                                                         in1=eyesb[:, 2 * P:3 * P], op0=ALU.mult, op1=ALU.add))
            A("tensor", lambda e: e.matmul(ps_a[:], VTt[:], Mt[:], start=True, stop=True))
            A("tensor", lambda e: e.matmul(ps_b[:], Mt[:], VTt[:], start=True, stop=True))
            A("scalar", lambda e: e.activation(out=Vt[:], in_=ps_a[:], func=AF.Identity, scale=0.25))
            A("scalar", lambda e: e.activation(out=VTt[:], in_=ps_b[:], func=AF.Identity, scale=0.25))
        A("sync", lambda e, m=mm: e.dma_start(out=z_o[m * P:(m + 1) * P, :], in_=Vt[:]), dma=True)
        A("sync", lambda e, m=mm: e.dma_start(out=zt_o[m * P:(m + 1) * P, :], in_=VTt[:]), dma=True)
    sp.emit(es)
    es.close()
    return nc



def build_D():
    nc = bass.Bass("TRN2", target_bir_lowering=False)
    q_in = nc.dram_tensor("q_fm", [NCH * P, T], BF16, kind="ExternalInput").ap()
    k_in = nc.dram_tensor("k_fm", [NCH * P, T], BF16, kind="ExternalInput").ap()
    v_in = nc.dram_tensor("v_tok", [T, D], BF16, kind="ExternalInput").ap()
    qlm_in = nc.dram_tensor("qlm", [NCH * P, BC * LM], BF16, kind="ExternalInput").ap()
    klm_in = nc.dram_tensor("klm", [NCH * P, BC * LM], BF16, kind="ExternalInput").ap()
    zt_in = nc.dram_tensor("zt", [NMAT * P, LM], BF16, kind="ExternalInput").ap()
    identf_in = nc.dram_tensor("identf", [P, P], F32, kind="ExternalInput").ap()
    attn_o = nc.dram_tensor("attn_tok", [T, D], F32, kind="ExternalOutput").ap()

    es = contextlib.ExitStack()
    sbt = lambda n, s, dt=F32: es.enter_context(nc.sbuf_tensor(n, s, dt))
    pst = lambda n, s, dt=F32: es.enter_context(nc.psum_tensor(n, s, dt))
    qlmt = [sbt(f"qlmt{c}", [P, BC * LM], BF16) for c in range(NCH)]
    klmt = [sbt(f"klmt{c}", [P, BC * LM], BF16) for c in range(NCH)]
    ksl = sbt("ksl", [P, S], BF16)
    qsl = sbt("qsl", [P, S], BF16)
    vsl = sbt("vsl", [P, 16 * HD], BF16)
    ztm = sbt("ztm", [P, LM], BF16)
    s3f = sbt("s3f", [P, S])
    e3 = sbt("e3", [P, S], BF16)
    e3t = es.enter_context(nc.sbuf_tensor("e3t", [P, S], BF16, is_transpose_dest=True))
    e1 = sbt("e1", [P, S], BF16)
    red = sbt("red", [P, 4])
    Us = sbt("Us", [P, HD], BF16)
    Ps = sbt("Ps", [P, HD], BF16)
    r1s = sbt("r1s", [1, S])
    rr1 = sbt("rr1", [1, S])
    bcs = sbt("bcs", [64, 512])
    onescb = sbt("onescb", [P, 1], BF16)
    onesr1 = sbt("onesr1", [1, 64])
    identf = sbt("identf_s", [P, P])
    afm = sbt("afm", [P, T])
    stag = sbt("stag", [P, P])
    ps_m = pst("ps_m", [P, 512])
    ps_u = pst("ps_u", [P, HD])
    ps_x = pst("ps_x", [64, 512])
    ps_bc = pst("ps_bc", [64, 512])
    ps_t = pst("ps_tD", [P, P])
    ps_r1 = pst("ps_r1", [1, 512])

    sp = SP(nc)
    A = sp.add
    for c in range(NCH):
        A("sync", lambda e, c=c: e.dma_start(out=qlmt[c][:], in_=qlm_in[c * P:(c + 1) * P, :]), dma=True)
        A("sync", lambda e, c=c: e.dma_start(out=klmt[c][:], in_=klm_in[c * P:(c + 1) * P, :]), dma=True)
    A("sync", lambda e: e.dma_start(out=identf[:], in_=identf_in), dma=True)
    A("gpsimd", lambda e: e.memset(onescb[:], 1.0))
    A("gpsimd", lambda e: e.memset(onesr1[:], 1.0))

    for cc in range(NCH):
        for s in range(BC):
            for hh in range(2):
                h = 2 * cc + hh
                m = s * HEADS + h
                A("sync", lambda e, cc=cc, hh=hh, s=s: e.dma_start(
                    out=ksl[64 * hh:64 * hh + 64, :],
                    in_=k_in[cc * P + 64 * hh: cc * P + 64 * hh + 64,
                             s * S:(s + 1) * S]), dma=True)
                A("sync", lambda e, cc=cc, hh=hh, s=s: e.dma_start(
                    out=qsl[64 * hh:64 * hh + 64, :],
                    in_=q_in[cc * P + 64 * hh: cc * P + 64 * hh + 64,
                             s * S:(s + 1) * S]), dma=True)
                for j in range(16):
                    A("sync", lambda e, s=s, j=j, h=h: e.dma_start(
                        out=vsl[:, j * HD:(j + 1) * HD],
                        in_=v_in[s * S + j * P: s * S + (j + 1) * P,
                                 h * HD:(h + 1) * HD]), dma=True)
                A("sync", lambda e, m=m: e.dma_start(out=ztm[:], in_=zt_in[m * P:(m + 1) * P, :]), dma=True)
                for b in range(4):
                    A("tensor", lambda e, cc=cc, hh=hh, s=s, b=b: e.matmul(
                        ps_m[:], qlmt[cc][64 * hh:64 * hh + 64, s * LM:(s + 1) * LM],
                        ksl[64 * hh:64 * hh + 64, b * 512:(b + 1) * 512], start=True, stop=True))
                    A("scalar", lambda e, b=b: e.activation(out=s3f[:, b * 512:(b + 1) * 512],
                                                            in_=ps_m[:], func=AF.Copy))
                A("vector", lambda e: e.tensor_reduce(out=red[:, 0:1], in_=s3f[:], axis=AX.X, op=ALU.max))
                A("vector", lambda e: e.tensor_scalar_mul(out=red[:, 1:2], in0=red[:, 0:1],
                                                          scalar1=-1.0 / SEG))
                A("scalar", lambda e: e.activation(out=e3[:], in_=s3f[:], func=AF.Exp,
                                                   scale=1.0 / SEG, bias=red[:, 1:2]))
                A("vector", lambda e: e.tensor_reduce(out=red[:, 2:3], in_=e3[:], axis=AX.X, op=ALU.add))
                A("vector", lambda e: e.reciprocal(out=red[:, 3:4], in_=red[:, 2:3]))
                for j in range(16):
                    A("scalar", lambda e, j=j: e.dma_start(out=e3t[:, j * P:(j + 1) * P],
                                                           in_=e3[:, j * P:(j + 1) * P],
                                                           transpose=True), dma=True)
                for j in range(16):
                    A("tensor", lambda e, j=j: e.matmul(ps_u[:], e3t[:, j * P:(j + 1) * P],
                                                        vsl[:, j * HD:(j + 1) * HD],
                                                        start=(j == 0), stop=(j == 15)))
                A("vector", lambda e: e.tensor_scalar(out=Us[:], in0=ps_u[:], scalar1=red[:, 3:4],
                                                      scalar2=None, op0=ALU.mult))
                A("tensor", lambda e: e.matmul(ps_u[:], ztm[:], Us[:], start=True, stop=True))
                A("scalar", lambda e: e.activation(out=Ps[:], in_=ps_u[:], func=AF.Copy))
                for b in range(4):
                    A("tensor", lambda e, cc=cc, hh=hh, s=s, b=b: e.matmul(
                        ps_m[:], klmt[cc][64 * hh:64 * hh + 64, s * LM:(s + 1) * LM],
                        qsl[64 * hh:64 * hh + 64, b * 512:(b + 1) * 512], start=True, stop=True))
                    A("scalar", lambda e, b=b: e.activation(out=e1[:, b * 512:(b + 1) * 512],
                                                            in_=ps_m[:], func=AF.Exp, scale=1.0 / SEG))
                for b in range(4):
                    A("tensor", lambda e, b=b: e.matmul(ps_r1[:], onescb[:], e1[:, b * 512:(b + 1) * 512],
                                                        start=True, stop=True))
                    A("scalar", lambda e, b=b: e.activation(out=r1s[0:1, b * 512:(b + 1) * 512],
                                                            in_=ps_r1[:], func=AF.Copy))
                A("vector", lambda e: e.reciprocal(out=rr1[:], in_=r1s[:]))
                for b in range(4):
                    A("tensor", lambda e, b=b: e.matmul(ps_x[:], Ps[:], e1[:, b * 512:(b + 1) * 512],
                                                        start=True, stop=True))
                    A("tensor", lambda e, b=b: e.matmul(ps_bc[:], onesr1[0:1, :],
                                                        rr1[0:1, b * 512:(b + 1) * 512],
                                                        start=True, stop=True))
                    A("scalar", lambda e, b=b: e.activation(out=bcs[:], in_=ps_bc[:], func=AF.Copy))
                    A("vector", lambda e, hh=hh, s=s, b=b: e.tensor_tensor(
                        out=afm[64 * hh:64 * hh + 64, s * S + b * 512: s * S + (b + 1) * 512],
                        in0=ps_x[:], in1=bcs[:], op=ALU.mult))
        for tt in range(NT):
            A("tensor", lambda e, tt=tt: e.transpose(ps_t[:], afm[:, tt * P:(tt + 1) * P], identf[:]))
            A("scalar", lambda e: e.activation(out=stag[:], in_=ps_t[:], func=AF.Copy))
            A("sync", lambda e, tt=tt, cc=cc: e.dma_start(
                out=attn_o[tt * P:(tt + 1) * P, cc * P:(cc + 1) * P], in_=stag[:]), dma=True)
    sp.emit(es)
    es.close()
    return nc



def emit_quant(Aop, x_ap, q_ap, scr, C, pp=P):
    nb = C // 16
    bm, sc, rs, hs = scr["bm"], scr["sc"], scr["rs"], scr["hs"]
    y, a, t1, t2, t3 = scr["y"], scr["a"], scr["t1"], scr["t2"], scr["t3"]
    ri1, ri2 = scr["ri1"], scr["ri2"]
    x3 = x_ap.rearrange("p (n w) -> p n w", w=16)
    Aop("vector", lambda e: e.tensor_reduce(out=bm[:pp, :nb], in_=x3, axis=AX.X,
                                            op=ALU.max, apply_absolute_value=True))
    Aop("vector", lambda e: e.tensor_scalar(out=sc[:pp, :nb], in0=bm[:pp, :nb], scalar1=1.0 / 6.0,
                                            scalar2=1e-12, op0=ALU.mult, op1=ALU.max))
    Aop("vector", lambda e: e.reciprocal(out=rs[:pp, :nb], in_=sc[:pp, :nb]))
    rs3 = rs[:pp, :nb].unsqueeze(2).broadcast_to([pp, nb, 16])
    y3 = y[:pp, :C].rearrange("p (n w) -> p n w", w=16)
    Aop("vector", lambda e: e.tensor_tensor(out=y3, in0=x3, in1=rs3, op=ALU.mult))
    Aop("vector", lambda e: e.tensor_scalar(out=a[:pp, :C].bitcast(mybir.dt.int32),
                                            in0=y[:pp, :C].bitcast(mybir.dt.int32),
                                            scalar1=0x7fffffff, scalar2=None,
                                            op0=ALU.bitwise_and))
    Aop("vector", lambda e: e.tensor_scalar_mul(out=ri1[:pp, :C], in0=a[:pp, :C], scalar1=2.0))
    Aop("vector", lambda e: e.tensor_copy(out=ri2[:pp, :C], in_=a[:pp, :C]))
    Aop("vector", lambda e: e.tensor_scalar(out=t1[:pp, :C], in0=ri1[:pp, :C], scalar1=0.5,
                                            scalar2=2.0, op0=ALU.mult, op1=ALU.min))
    Aop("vector", lambda e: e.tensor_scalar(out=t2[:pp, :C], in0=ri2[:pp, :C], scalar1=2.0,
                                            scalar2=4.0, op0=ALU.max, op1=ALU.min))
    Aop("vector", lambda e: e.tensor_tensor(out=t1[:pp, :C], in0=t1[:pp, :C], in1=t2[:pp, :C],
                                            op=ALU.add))
    Aop("vector", lambda e: e.tensor_scalar(out=t2[:pp, :C], in0=a[:pp, :C], scalar1=5.0,
                                            scalar2=2.0, op0=ALU.is_gt, op1=ALU.mult))
    Aop("vector", lambda e: e.scalar_tensor_tensor(out=t1[:pp, :C], in0=t1[:pp, :C], scalar=-2.0,
                                                   in1=t2[:pp, :C], op0=ALU.add, op1=ALU.add))
    Aop("scalar", lambda e: e.activation(out=t3[:pp, :C], in_=y[:pp, :C], func=AF.Sign))
    Aop("vector", lambda e: e.tensor_tensor(out=t1[:pp, :C], in0=t1[:pp, :C], in1=t3[:pp, :C],
                                            op=ALU.mult))
    Aop("vector", lambda e: e.tensor_scalar_mul(out=hs[:pp, :nb], in0=sc[:pp, :nb], scalar1=1.0))
    hs3 = hs[:pp, :nb].unsqueeze(2).broadcast_to([pp, nb, 16])
    t13 = t1[:pp, :C].rearrange("p (n w) -> p n w", w=16)
    q3 = q_ap.rearrange("p (n w) -> p n w", w=16)
    Aop("vector", lambda e: e.tensor_tensor(out=q3, in0=t13, in1=hs3, op=ALU.mult))


def quant_scratch(sbt, tag, C, dt_mid=F32):
    nb = C // 16
    return {
        "bm": sbt(f"bm{tag}", [P, nb]), "sc": sbt(f"sc{tag}", [P, nb]),
        "rs": sbt(f"rs{tag}", [P, nb]), "hs": sbt(f"hs{tag}", [P, nb]),
        "y": sbt(f"yq{tag}", [P, C]), "a": sbt(f"aq{tag}", [P, C]),
        "t1": sbt(f"t1q{tag}", [P, C], BF16), "t2": sbt(f"t2q{tag}", [P, C], BF16),
        "t3": sbt(f"t3q{tag}", [P, C], BF16),
        "ri1": sbt(f"ri1q{tag}", [P, C], I16), "ri2": sbt(f"ri2q{tag}", [P, C], I16),
    }



def build_WQ():
    nc = bass.Bass("TRN2", target_bir_lowering=False)
    wg_in = nc.dram_tensor("wg_sh", [384, D], F32, kind="ExternalInput").ap()
    wu_in = nc.dram_tensor("wu_sh", [384, D], F32, kind="ExternalInput").ap()
    wd_in = nc.dram_tensor("wd_sh", [P, I_FFN], F32, kind="ExternalInput").ap()
    wgT_o = nc.dram_tensor("wgqT_sh", [D, 384], BF16, kind="ExternalOutput").ap()
    wuT_o = nc.dram_tensor("wuqT_sh", [D, 384], BF16, kind="ExternalOutput").ap()
    wdT_o = nc.dram_tensor("wdqT_sh", [I_FFN, P], BF16, kind="ExternalOutput").ap()

    es = contextlib.ExitStack()
    sbt = lambda n, s, dt=F32: es.enter_context(nc.sbuf_tensor(n, s, dt))
    wt = sbt("wt", [P, I_FFN])
    qt = sbt("qt", [P, I_FFN], BF16)
    qtt = es.enter_context(nc.sbuf_tensor("qtt", [P, I_FFN], BF16, is_transpose_dest=True))
    scr = quant_scratch(sbt, "w", I_FFN)
    sp = SP(nc)
    A = sp.add
    jobs = []
    for r in range(3):
        jobs.append((wg_in[r * P:(r + 1) * P, :], wgT_o, r, D))
        jobs.append((wu_in[r * P:(r + 1) * P, :], wuT_o, r, D))
    jobs.append((wd_in, wdT_o, 0, I_FFN))
    for (src_ap, dst, r, C) in jobs:
        A("sync", lambda e, src_ap=src_ap, C=C: e.dma_start(out=wt[:, 0:C], in_=src_ap), dma=True)
        emit_quant(A, wt[:, 0:C], qt[:, 0:C], scr, C)
        for j in range(C // P):
            A("scalar", lambda e, j=j: e.dma_start(out=qtt[:, j * P:(j + 1) * P],
                                                   in_=qt[:, j * P:(j + 1) * P],
                                                   transpose=True), dma=True)
        for j in range(C // P):
            A("sync", lambda e, dst=dst, r=r, j=j: e.dma_start(
                out=dst[j * P:(j + 1) * P, r * P:(r + 1) * P],
                in_=qtt[:, j * P:(j + 1) * P]), dma=True)
    sp.emit(es)
    es.close()
    return nc



def build_FFN():
    nc = bass.Bass("TRN2", target_bir_lowering=False)
    xn_in = nc.dram_tensor("xn_tok", [T, D], F32, kind="ExternalInput").ap()
    wg_in = nc.dram_tensor("wgq_fm", [D, I_FFN], BF16, kind="ExternalInput").ap()
    wu_in = nc.dram_tensor("wuq_fm", [D, I_FFN], BF16, kind="ExternalInput").ap()
    wd_in = nc.dram_tensor("wdqT", [I_FFN, D], BF16, kind="ExternalInput").ap()
    ffn_o = nc.dram_tensor("ffn_tok", [T, D], F32, kind="ExternalOutput").ap()
    hq_sc = nc.dram_tensor("hq_sc", [I_FFN, T], BF16, kind="Internal").ap()

    es = contextlib.ExitStack()
    sbt = lambda n, s, dt=F32: es.enter_context(nc.sbuf_tensor(n, s, dt))
    pst = lambda n, s, dt=F32: es.enter_context(nc.psum_tensor(n, s, dt))
    wpool = sbt("wpool", [P, 2 * NCH * I_FFN], BF16)
    wgq = [wpool[:, c * I_FFN:(c + 1) * I_FFN] for c in range(NCH)]
    wuq = [wpool[:, (NCH + c) * I_FFN:(NCH + c + 1) * I_FFN] for c in range(NCH)]
    wdq = [wpool[:, j * D:(j + 1) * D] for j in range(22)]
    xt = sbt("xt", [P, D])
    xq = sbt("xq", [P, D], BF16)
    xqf = es.enter_context(nc.sbuf_tensor("xqf", [P, D], BF16, is_transpose_dest=True))
    gact = sbt("gact", [P, I_FFN])
    uact = sbt("uact", [P, I_FFN])
    hact = sbt("hact", [P, I_FFN])
    hqt = sbt("hqt", [P, I_FFN], BF16)
    hqtt = es.enter_context(nc.sbuf_tensor("hqtt", [P, I_FFN], BF16, is_transpose_dest=True))
    scr = quant_scratch(sbt, "f", I_FFN)
    hload = sbt("hload", [P, P], BF16)
    ot = sbt("ot", [P, D])
    ps = [pst(f"psF{i}", [P, 512]) for i in range(2)]

    IB = [(0, 512), (512, 1024), (1024, 1536), (1536, 2048), (2048, 2560), (2560, 2816)]
    sp = SP(nc)
    A = sp.add
    for c in range(NCH):
        A("sync", lambda e, c=c: e.dma_start(out=wgq[c], in_=wg_in[c * P:(c + 1) * P, :]), dma=True)
        A("sync", lambda e, c=c: e.dma_start(out=wuq[c], in_=wu_in[c * P:(c + 1) * P, :]), dma=True)
    for tt in range(NT):
        A("sync", lambda e, tt=tt: e.dma_start(out=xt[:], in_=xn_in[tt * P:(tt + 1) * P, :]), dma=True)
        emit_quant(A, xt[:], xq[:], scr, D)
        for j in range(NCH):
            A("scalar", lambda e, j=j: e.dma_start(out=xqf[:, j * P:(j + 1) * P],
                                                   in_=xq[:, j * P:(j + 1) * P], transpose=True), dma=True)
        for wsel in range(2):
            wts = (wgq, wuq)[wsel]
            dstact = (gact, uact)[wsel]
            for (lo, hi) in IB:
                n = hi - lo
                for cc in range(NCH):
                    A("tensor", lambda e, cc=cc, lo=lo, hi=hi, wts=wts, n=n: e.matmul(
                        ps[0][:, 0:n], xqf[:, cc * P:(cc + 1) * P], wts[cc][:, lo:hi],
                        start=(cc == 0), stop=(cc == NCH - 1)))
                if wsel == 0:
                    A("scalar", lambda e, lo=lo, hi=hi, n=n, dstact=dstact: e.activation(
                        out=dstact[:, lo:hi], in_=ps[0][:, 0:n], func=AF.Silu))
                else:
                    A("scalar", lambda e, lo=lo, hi=hi, n=n, dstact=dstact: e.activation(
                        out=dstact[:, lo:hi], in_=ps[0][:, 0:n], func=AF.Copy))
        A("vector", lambda e: e.tensor_tensor(out=hact[:], in0=gact[:], in1=uact[:], op=ALU.mult))
        emit_quant(A, hact[:], hqt[:], scr, I_FFN)
        for j in range(22):
            A("scalar", lambda e, j=j: e.dma_start(out=hqtt[:, j * P:(j + 1) * P],
                                                   in_=hqt[:, j * P:(j + 1) * P], transpose=True), dma=True)
        for j in range(22):
            A("sync", lambda e, j=j, tt=tt: e.dma_start(
                out=hq_sc[j * P:(j + 1) * P, tt * P:(tt + 1) * P],
                in_=hqtt[:, j * P:(j + 1) * P]), dma=True)
    for j in range(22):
        A("sync", lambda e, j=j: e.dma_start(out=wdq[j], in_=wd_in[j * P:(j + 1) * P, :]), dma=True)
    for tt in range(NT):
        for half in range(2):
            for j in range(22):
                A("sync", lambda e, j=j, tt=tt: e.dma_start(
                    out=hload[:], in_=hq_sc[j * P:(j + 1) * P, tt * P:(tt + 1) * P]), dma=True)
                A("tensor", lambda e, j=j, half=half: e.matmul(
                    ps[half][:], hload[:], wdq[j][:, half * 512:(half + 1) * 512],
                    start=(j == 0), stop=(j == 21)))
            A("scalar", lambda e, half=half: e.activation(out=ot[:, half * 512:(half + 1) * 512],
                                                          in_=ps[half][:], func=AF.Copy))
        A("sync", lambda e, tt=tt: e.dma_start(out=ffn_o[tt * P:(tt + 1) * P, :], in_=ot[:]), dma=True)
    sp.emit(es)
    es.close()
    return nc



def build_F():
    nc = bass.Bass("TRN2", target_bir_lowering=False)
    x_in = nc.dram_tensor("x", [T, D], F32, kind="ExternalInput").ap()
    at_in = nc.dram_tensor("attn_tok", [T, D], F32, kind="ExternalInput").ap()
    ff_in = nc.dram_tensor("ffn_tok", [T, D], F32, kind="ExternalInput").ap()
    eps_in = nc.dram_tensor("epsc", [P, 1], F32, kind="ExternalInput").ap()
    x1_o = nc.dram_tensor("x1_tok", [T, D], F32, kind="ExternalOutput").ap()
    xn1_o = nc.dram_tensor("xn1_tok", [T, D], F32, kind="ExternalOutput").ap()

    es = contextlib.ExitStack()
    sbt = lambda n, s, dt=F32: es.enter_context(nc.sbuf_tensor(n, s, dt))
    xa = sbt("xa", [P, D]); xb = sbt("xb", [P, D]); xc = sbt("xc", [P, D])
    xn = sbt("xn", [P, D]); xsq = sbt("xsqF", [P, D])
    stat = sbt("statF", [P, 8]); epsc = sbt("epscF", [P, 1])
    sp = SP(nc)
    A = sp.add
    A("sync", lambda e: e.dma_start(out=epsc[:], in_=eps_in), dma=True)
    for i in range(NT):
        A("sync", lambda e, i=i: e.dma_start(out=xa[:], in_=x_in[i * P:(i + 1) * P, :]), dma=True)
        A("sync", lambda e, i=i: e.dma_start(out=xb[:], in_=at_in[i * P:(i + 1) * P, :]), dma=True)
        A("sync", lambda e, i=i: e.dma_start(out=xc[:], in_=ff_in[i * P:(i + 1) * P, :]), dma=True)
        A("vector", lambda e: e.tensor_tensor(out=xa[:], in0=xa[:], in1=xb[:], op=ALU.add))
        A("vector", lambda e: e.tensor_tensor(out=xa[:], in0=xa[:], in1=xc[:], op=ALU.add))
        A("sync", lambda e, i=i: e.dma_start(out=x1_o[i * P:(i + 1) * P, :], in_=xa[:]), dma=True)
        A("vector", lambda e: e.tensor_reduce(out=stat[:, 0:1], in_=xa[:], axis=AX.X, op=ALU.add))
        A("vector", lambda e: e.tensor_tensor(out=xsq[:], in0=xa[:], in1=xa[:], op=ALU.mult))
        A("vector", lambda e: e.tensor_reduce(out=stat[:, 1:2], in_=xsq[:], axis=AX.X, op=ALU.add))
        A("vector", lambda e: e.tensor_scalar_mul(out=stat[:, 2:3], in0=stat[:, 0:1], scalar1=1.0 / D))
        A("vector", lambda e: e.tensor_scalar_mul(out=stat[:, 3:4], in0=stat[:, 1:2], scalar1=1.0 / D))
        A("vector", lambda e: e.tensor_tensor(out=stat[:, 4:5], in0=stat[:, 2:3], in1=stat[:, 2:3], op=ALU.mult))
        A("vector", lambda e: e.tensor_tensor(out=stat[:, 4:5], in0=stat[:, 3:4], in1=stat[:, 4:5], op=ALU.subtract))
        A("scalar", lambda e: e.activation(out=stat[:, 5:6], in_=stat[:, 4:5], func=AF.Ln, bias=epsc[:, 0:1]))
        A("scalar", lambda e: e.activation(out=stat[:, 5:6], in_=stat[:, 5:6], func=AF.Exp, scale=-0.5))
        A("vector", lambda e: e.scalar_tensor_tensor(out=stat[:, 6:7], in0=stat[:, 2:3], scalar=-1.0,
                                                     in1=stat[:, 5:6], op0=ALU.mult, op1=ALU.mult))
        A("scalar", lambda e: e.activation(out=xn[:], in_=xa[:], func=AF.Identity,
                                           scale=stat[:, 5:6], bias=stat[:, 6:7]))
        A("sync", lambda e, i=i: e.dma_start(out=xn1_o[i * P:(i + 1) * P, :], in_=xn[:]), dma=True)
    sp.emit(es)
    es.close()
    return nc



def build_I():
    nc = bass.Bass("TRN2", target_bir_lowering=False)
    a_in = nc.dram_tensor("x1_tok", [T, D], F32, kind="ExternalInput").ap()
    b_in = nc.dram_tensor("attn1_tok", [T, D], F32, kind="ExternalInput").ap()
    c_in = nc.dram_tensor("ffn1_tok", [T, D], F32, kind="ExternalInput").ap()
    o_out = nc.dram_tensor("out", [T, D], F32, kind="ExternalOutput").ap()
    es = contextlib.ExitStack()
    sbt = lambda n, s, dt=F32: es.enter_context(nc.sbuf_tensor(n, s, dt))
    xa = sbt("xaI", [P, D]); xb = sbt("xbI", [P, D]); xc = sbt("xcI", [P, D])
    sp = SP(nc)
    A = sp.add
    for i in range(NT):
        A("sync", lambda e, i=i: e.dma_start(out=xa[:], in_=a_in[i * P:(i + 1) * P, :]), dma=True)
        A("sync", lambda e, i=i: e.dma_start(out=xb[:], in_=b_in[i * P:(i + 1) * P, :]), dma=True)
        A("sync", lambda e, i=i: e.dma_start(out=xc[:], in_=c_in[i * P:(i + 1) * P, :]), dma=True)
        A("vector", lambda e: e.tensor_tensor(out=xa[:], in0=xa[:], in1=xb[:], op=ALU.add))
        A("vector", lambda e: e.tensor_tensor(out=xa[:], in0=xa[:], in1=xc[:], op=ALU.add))
        A("sync", lambda e, i=i: e.dma_start(out=o_out[i * P:(i + 1) * P, :], in_=xa[:]), dma=True)
    sp.emit(es)
    es.close()
    return nc



def build_G():
    nc = bass.Bass("TRN2", target_bir_lowering=False)
    inp = lambda n, s, dt=F32: nc.dram_tensor(n, s, dt, kind="ExternalInput").ap()
    xn1_in = inp("xn1_tok", [T, D])
    ln_b = {n: inp(n, [P, D]) for n in ("ln0w", "ln0b", "ln1w", "ln1b", "ln2w", "ln2b",
                                        "tmkb", "tmvb", "tmrb", "cmkb", "cmrb")}
    wkT_in = inp("wkT", [D, D]); wvT_in = inp("wvT", [D, D]); wrT_in = inp("wrT", [D, D])
    woT_in = inp("woT", [D, D]); wfkT_in = inp("wfkT", [D, H_FFN])
    wfvT_in = inp("wfvT", [H_FFN, D]); wfrT_in = inp("wfrT", [D, D])
    tdc_in = inp("tdc", [P, NCH]); tfc_in = inp("tfc", [P, NCH])
    eps_in = inp("epsc", [P, 1])
    attn1_o = nc.dram_tensor("attn1_tok", [T, D], F32, kind="ExternalOutput").ap()
    a_sc = nc.dram_tensor("a_sc", [T, D], F32, kind="Internal").ap()
    x2_sc = nc.dram_tensor("x2_sc", [T, D], F32, kind="Internal").ap()
    xk_sc = nc.dram_tensor("xk_sc", [D, T], BF16, kind="Internal").ap()
    xv_sc = nc.dram_tensor("xv_sc", [D, T], BF16, kind="Internal").ap()
    xr_sc = nc.dram_tensor("xr_sc", [D, T], BF16, kind="Internal").ap()
    xr2_sc = nc.dram_tensor("xr2_sc", [D, T], BF16, kind="Internal").ap()
    sa_sc = nc.dram_tensor("sa_sc", [D, T], BF16, kind="Internal").ap()
    kfq_sc = nc.dram_tensor("kfq_sc", [H_FFN, T], BF16, kind="Internal").ap()

    es = contextlib.ExitStack()
    sbt = lambda n, s, dt=F32: es.enter_context(nc.sbuf_tensor(n, s, dt))
    pst = lambda n, s, dt=F32: es.enter_context(nc.psum_tensor(n, s, dt))
    pool = sbt("poolG", [P, 25000])
    xk2_fm = [sbt(f"xk2fm{c}", [P, T], BF16) for c in range(NCH)]
    trd = es.enter_context(nc.sbuf_tensor("trdG", [P, D], BF16, is_transpose_dest=True))
    stat = sbt("statG", [P, 8])
    epsc = sbt("epscG", [P, 1])
    smallc = sbt("smallc", [P, 8])
    tdc = sbt("tdcG", [P, NCH]); tfc = sbt("tfcG", [P, NCH])
    hlast = sbt("hlastG", [1, D])
    ps = [pst(f"psG{i}", [P, 512]) for i in range(2)]

    class Carve:
        def __init__(self, base):
            self.base = base
            self.off = 0
        def f32(self, n):
            v = pool[:, self.off:self.off + n]
            self.off += n
            return v
        def bf(self, n):
            v = pool[:, self.off:self.off + (n + 1) // 2].bitcast(BF16)
            self.off += (n + 1) // 2
            return v

    sp = SP(nc)
    A = sp.add
    A("sync", lambda e: e.dma_start(out=epsc[:], in_=eps_in), dma=True)
    A("sync", lambda e: e.dma_start(out=tdc[:], in_=tdc_in), dma=True)
    A("sync", lambda e: e.dma_start(out=tfc[:], in_=tfc_in), dma=True)

    def emit_stats(xa):
        A("vector", lambda e: e.tensor_reduce(out=stat[:, 0:1], in_=xa, axis=AX.X, op=ALU.add))
        A("vector", lambda e: e.tensor_tensor(out=stat[:, 7:8], in0=stat[:, 0:1], in1=stat[:, 0:1], op=ALU.bypass))
        A("vector", lambda e: e.tensor_scalar_mul(out=stat[:, 2:3], in0=stat[:, 0:1], scalar1=1.0 / D))
    def emit_ln(xa, dst, wname, bname, xsq):
        A("vector", lambda e: e.tensor_reduce(out=stat[:, 0:1], in_=xa, axis=AX.X, op=ALU.add))
        A("vector", lambda e: e.tensor_tensor(out=xsq, in0=xa, in1=xa, op=ALU.mult))
        A("vector", lambda e: e.tensor_reduce(out=stat[:, 1:2], in_=xsq, axis=AX.X, op=ALU.add))
        A("vector", lambda e: e.tensor_scalar_mul(out=stat[:, 2:3], in0=stat[:, 0:1], scalar1=1.0 / D))
        A("vector", lambda e: e.tensor_scalar_mul(out=stat[:, 3:4], in0=stat[:, 1:2], scalar1=1.0 / D))
        A("vector", lambda e: e.tensor_tensor(out=stat[:, 4:5], in0=stat[:, 2:3], in1=stat[:, 2:3], op=ALU.mult))
        A("vector", lambda e: e.tensor_tensor(out=stat[:, 4:5], in0=stat[:, 3:4], in1=stat[:, 4:5], op=ALU.subtract))
        A("scalar", lambda e: e.activation(out=stat[:, 5:6], in_=stat[:, 4:5], func=AF.Ln, bias=epsc[:, 0:1]))
        A("scalar", lambda e: e.activation(out=stat[:, 5:6], in_=stat[:, 5:6], func=AF.Exp, scale=-0.5))
        A("vector", lambda e: e.scalar_tensor_tensor(out=stat[:, 6:7], in0=stat[:, 2:3], scalar=-1.0,
                                                     in1=stat[:, 5:6], op0=ALU.mult, op1=ALU.mult))
        A("scalar", lambda e: e.activation(out=dst, in_=xa, func=AF.Identity,
                                           scale=stat[:, 5:6], bias=stat[:, 6:7]))
        A("vector", lambda e, wname=wname: e.tensor_tensor(out=dst, in0=dst, in1=lnt[wname][:], op=ALU.mult))
        A("vector", lambda e, bname=bname: e.tensor_tensor(out=dst, in0=dst, in1=lnt[bname][:], op=ALU.add))

    cv = Carve(0)
    lnt = {n: sbt(f"lnt_{n}", [P, D]) for n in ("ln0w", "ln0b", "ln1w", "ln1b", "tmkb", "tmvb", "tmrb")}
    for n in lnt:
        A("sync", lambda e, n=n: e.dma_start(out=lnt[n][:], in_=ln_b[n]), dma=True)
    xt = cv.f32(D); at = cv.f32(D); ht = cv.f32(D); xxt = cv.f32(D)
    dt_ = cv.f32(D); mixt = cv.f32(D); xsq1 = cv.f32(D)
    mixb = cv.bf(D)
    ph1_top = cv.off
    for i in range(NT):
        A("sync", lambda e, i=i: e.dma_start(out=xt, in_=xn1_in[i * P:(i + 1) * P, :]), dma=True)
        emit_ln(xt, at, "ln0w", "ln0b", xsq1)
        A("sync", lambda e, i=i: e.dma_start(out=a_sc[i * P:(i + 1) * P, :], in_=at), dma=True)
        emit_ln(at, ht, "ln1w", "ln1b", xsq1)
        if i % 16 == 0:
            A("gpsimd", lambda e: e.memset(xxt[0:1, 0:D], 0.0))
        else:
            A("sync", lambda e: e.dma_start(out=xxt[0:1, 0:D], in_=hlast[:]), dma=True)
        A("sync", lambda e: e.dma_start(out=xxt[1:P, 0:D], in_=ht[0:P - 1, 0:D]), dma=True)
        A("sync", lambda e: e.dma_start(out=hlast[:], in_=ht[P - 1:P, 0:D]), dma=True)
        A("vector", lambda e: e.tensor_tensor(out=dt_, in0=ht, in1=xxt, op=ALU.subtract))
        for (mn, dst_sc) in (("tmkb", xk_sc), ("tmvb", xv_sc), ("tmrb", xr_sc)):
            A("vector", lambda e, mn=mn: e.tensor_tensor(out=mixt, in0=dt_, in1=lnt[mn][:], op=ALU.mult))
            A("vector", lambda e: e.tensor_tensor(out=mixt, in0=mixt, in1=xxt, op=ALU.add))
            A("vector", lambda e: e.tensor_copy(out=mixb, in_=mixt))
            for c in range(NCH):
                A("scalar", lambda e, c=c: e.dma_start(out=trd[:, c * P:(c + 1) * P],
                                                       in_=mixb[:, c * P:(c + 1) * P],
                                                       transpose=True), dma=True)
            for c in range(NCH):
                A("sync", lambda e, c=c, i=i, dst_sc=dst_sc: e.dma_start(
                    out=dst_sc[c * P:(c + 1) * P, i * P:(i + 1) * P],
                    in_=trd[:, c * P:(c + 1) * P]), dma=True)

    cv = Carve(0)
    ko = cv.f32(S); vo = cv.f32(S); sro = cv.f32(S)
    eko = cv.f32(S); ekvo = cv.f32(S)
    Asc = cv.f32(S + 1); Bsc = cv.f32(S + 1)
    numt = cv.f32(S); dent = cv.f32(S); rdt = cv.f32(S)
    wblk = [cv.bf(P) for _ in range(24)]
    rhsb = cv.bf(512)
    saob = cv.bf(S)
    wld2 = cv.f32(P)
    for oc in range(NCH):
        for pi, w_in in enumerate((wkT_in, wvT_in, wrT_in)):
            for cc in range(NCH):
                A("sync", lambda e, w_in=w_in, cc=cc, oc=oc: e.dma_start(
                    out=wld2, in_=w_in[cc * P:(cc + 1) * P, oc * P:(oc + 1) * P]), dma=True)
                A("vector", lambda e, pi=pi, cc=cc: e.tensor_copy(out=wblk[pi * NCH + cc], in_=wld2))
        A("scalar", lambda e, oc=oc: e.activation(out=smallc[:, 0:1], in_=tdc[:, oc:oc + 1], func=AF.Exp))
        A("scalar", lambda e: e.activation(out=smallc[:, 1:2], in_=smallc[:, 0:1], func=AF.Exp, scale=-1.0))
        A("scalar", lambda e, oc=oc: e.activation(out=smallc[:, 2:3], in_=tfc[:, oc:oc + 1], func=AF.Exp))
        for s in range(BC):
            for (pi, src_sc, dsto, fn) in ((0, xk_sc, ko, AF.Copy), (1, xv_sc, vo, AF.Copy),
                                           (2, xr_sc, sro, AF.Sigmoid)):
                for tb in range(4):
                    for cc in range(NCH):
                        A("sync", lambda e, src_sc=src_sc, cc=cc, s=s, tb=tb: e.dma_start(
                            out=rhsb, in_=src_sc[cc * P:(cc + 1) * P,
                                                 s * S + tb * 512: s * S + (tb + 1) * 512]), dma=True)
                        A("tensor", lambda e, pi=pi, cc=cc: e.matmul(
                            ps[0][:], wblk[pi * NCH + cc], rhsb, start=(cc == 0), stop=(cc == NCH - 1)))
                    A("scalar", lambda e, dsto=dsto, fn=fn, tb=tb: e.activation(
                        out=dsto[:, tb * 512:(tb + 1) * 512], in_=ps[0][:], func=fn))
            A("scalar", lambda e: e.activation(out=eko, in_=ko, func=AF.Exp))
            A("vector", lambda e: e.tensor_tensor(out=ekvo, in0=eko, in1=vo, op=ALU.mult))
            A("gpsimd", lambda e: e.memset(Asc[:, 0:1], 0.0))
            A("gpsimd", lambda e: e.memset(Bsc[:, 0:1], 0.0))
            ewb = smallc[:, 1:2].broadcast_to([P, S])
            A("vector", lambda e: e.tensor_tensor_scan(out=Asc[:, 1:S + 1], data0=ewb, data1=ekvo,
                                                       initial=0.0, op0=ALU.mult, op1=ALU.add))
            A("vector", lambda e: e.tensor_tensor_scan(out=Bsc[:, 1:S + 1], data0=ewb, data1=eko,
                                                       initial=0.0, op0=ALU.mult, op1=ALU.add))
            A("vector", lambda e: e.scalar_tensor_tensor(out=numt, in0=ekvo, scalar=smallc[:, 2:3],
                                                         in1=Asc[:, 0:S], op0=ALU.mult, op1=ALU.add))
            A("vector", lambda e: e.scalar_tensor_tensor(out=dent, in0=eko, scalar=smallc[:, 2:3],
                                                         in1=Bsc[:, 0:S], op0=ALU.mult, op1=ALU.add))
            A("scalar", lambda e: e.activation(out=rdt, in_=dent, func=AF.Ln))
            A("scalar", lambda e: e.activation(out=rdt, in_=rdt, func=AF.Exp, scale=-1.0))
            A("vector", lambda e: e.tensor_tensor(out=numt, in0=numt, in1=rdt, op=ALU.mult))
            A("vector", lambda e: e.tensor_tensor(out=saob, in0=numt, in1=sro, op=ALU.mult))
            A("sync", lambda e, oc=oc, s=s: e.dma_start(
                out=sa_sc[oc * P:(oc + 1) * P, s * S:(s + 1) * S], in_=saob), dma=True)

    cv = Carve(0)
    lnt3 = {}
    for n in ("ln2w", "ln2b", "cmkb", "cmrb"):
        t = cv.f32(D)
        lnt3[n] = t
        A("sync", lambda e, n=n, t=t: e.dma_start(out=t, in_=ln_b[n]), dma=True)
    lnt.update({k: type("W", (), {"__getitem__": (lambda self, idx, t=v: t)})() for k, v in lnt3.items()})
    woTb = [cv.bf(D) for _ in range(NCH)]
    wld3 = cv.f32(D)
    for cc in range(NCH):
        A("sync", lambda e, cc=cc: e.dma_start(out=wld3, in_=woT_in[cc * P:(cc + 1) * P, :]), dma=True)
        A("vector", lambda e, cc=cc: e.tensor_copy(out=woTb[cc], in_=wld3))
    salhs = cv.bf(P)
    ot3 = cv.f32(D); a3 = cv.f32(D); x23 = cv.f32(D); h23 = cv.f32(D)
    xx3 = cv.f32(D); d3 = cv.f32(D); mix3 = cv.f32(D); xsq3 = cv.f32(D)
    mixb3 = cv.bf(D)
    for i in range(NT):
        for half in range(2):
            for cc in range(NCH):
                A("sync", lambda e, cc=cc, i=i: e.dma_start(
                    out=salhs, in_=sa_sc[cc * P:(cc + 1) * P, i * P:(i + 1) * P]), dma=True)
                A("tensor", lambda e, cc=cc, half=half: e.matmul(
                    ps[0][:], salhs, woTb[cc][:, half * 512:(half + 1) * 512],
                    start=(cc == 0), stop=(cc == NCH - 1)))
            A("scalar", lambda e, half=half: e.activation(out=ot3[:, half * 512:(half + 1) * 512],
                                                          in_=ps[0][:], func=AF.Copy))
        A("sync", lambda e, i=i: e.dma_start(out=a3, in_=a_sc[i * P:(i + 1) * P, :]), dma=True)
        A("vector", lambda e: e.tensor_tensor(out=x23, in0=ot3, in1=a3, op=ALU.add))
        A("sync", lambda e, i=i: e.dma_start(out=x2_sc[i * P:(i + 1) * P, :], in_=x23), dma=True)
        A("vector", lambda e: e.tensor_reduce(out=stat[:, 0:1], in_=x23, axis=AX.X, op=ALU.add))
        A("vector", lambda e: e.tensor_tensor(out=xsq3, in0=x23, in1=x23, op=ALU.mult))
        A("vector", lambda e: e.tensor_reduce(out=stat[:, 1:2], in_=xsq3, axis=AX.X, op=ALU.add))
        A("vector", lambda e: e.tensor_scalar_mul(out=stat[:, 2:3], in0=stat[:, 0:1], scalar1=1.0 / D))
        A("vector", lambda e: e.tensor_scalar_mul(out=stat[:, 3:4], in0=stat[:, 1:2], scalar1=1.0 / D))
        A("vector", lambda e: e.tensor_tensor(out=stat[:, 4:5], in0=stat[:, 2:3], in1=stat[:, 2:3], op=ALU.mult))
        A("vector", lambda e: e.tensor_tensor(out=stat[:, 4:5], in0=stat[:, 3:4], in1=stat[:, 4:5], op=ALU.subtract))
        A("scalar", lambda e: e.activation(out=stat[:, 5:6], in_=stat[:, 4:5], func=AF.Ln, bias=epsc[:, 0:1]))
        A("scalar", lambda e: e.activation(out=stat[:, 5:6], in_=stat[:, 5:6], func=AF.Exp, scale=-0.5))
        A("vector", lambda e: e.scalar_tensor_tensor(out=stat[:, 6:7], in0=stat[:, 2:3], scalar=-1.0,
                                                     in1=stat[:, 5:6], op0=ALU.mult, op1=ALU.mult))
        A("scalar", lambda e: e.activation(out=h23, in_=x23, func=AF.Identity,
                                           scale=stat[:, 5:6], bias=stat[:, 6:7]))
        A("vector", lambda e: e.tensor_tensor(out=h23, in0=h23, in1=lnt3["ln2w"], op=ALU.mult))
        A("vector", lambda e: e.tensor_tensor(out=h23, in0=h23, in1=lnt3["ln2b"], op=ALU.add))
        if i % 16 == 0:
            A("gpsimd", lambda e: e.memset(xx3[0:1, 0:D], 0.0))
        else:
            A("sync", lambda e: e.dma_start(out=xx3[0:1, 0:D], in_=hlast[:]), dma=True)
        A("sync", lambda e: e.dma_start(out=xx3[1:P, 0:D], in_=h23[0:P - 1, 0:D]), dma=True)
        A("sync", lambda e: e.dma_start(out=hlast[:], in_=h23[P - 1:P, 0:D]), dma=True)
        A("vector", lambda e: e.tensor_tensor(out=d3, in0=h23, in1=xx3, op=ALU.subtract))
        A("vector", lambda e: e.tensor_tensor(out=mix3, in0=d3, in1=lnt3["cmkb"], op=ALU.mult))
        A("vector", lambda e: e.tensor_tensor(out=mix3, in0=mix3, in1=xx3, op=ALU.add))
        A("vector", lambda e: e.tensor_copy(out=mixb3, in_=mix3))
        for c in range(NCH):
            A("scalar", lambda e, c=c: e.dma_start(out=trd[:, c * P:(c + 1) * P],
                                                   in_=mixb3[:, c * P:(c + 1) * P], transpose=True), dma=True)
        for c in range(NCH):
            A("vector", lambda e, c=c, i=i: e.tensor_copy(out=xk2_fm[c][:, i * P:(i + 1) * P],
                                                          in_=trd[:, c * P:(c + 1) * P]))
        A("vector", lambda e: e.tensor_tensor(out=mix3, in0=d3, in1=lnt3["cmrb"], op=ALU.mult))
        A("vector", lambda e: e.tensor_tensor(out=mix3, in0=mix3, in1=xx3, op=ALU.add))
        A("vector", lambda e: e.tensor_copy(out=mixb3, in_=mix3))
        for c in range(NCH):
            A("scalar", lambda e, c=c: e.dma_start(out=trd[:, c * P:(c + 1) * P],
                                                   in_=mixb3[:, c * P:(c + 1) * P], transpose=True), dma=True)
        for c in range(NCH):
            A("sync", lambda e, c=c, i=i: e.dma_start(
                out=xr2_sc[c * P:(c + 1) * P, i * P:(i + 1) * P],
                in_=trd[:, c * P:(c + 1) * P]), dma=True)

    cv = Carve(0)
    wfkb = [cv.bf(P) for _ in range(NCH)]
    wld4 = cv.f32(P)
    kfo = cv.f32(T)
    kfqo = cv.bf(T)
    for o4 in range(H_FFN // P):
        for cc in range(NCH):
            A("sync", lambda e, cc=cc, o4=o4: e.dma_start(
                out=wld4, in_=wfkT_in[cc * P:(cc + 1) * P, o4 * P:(o4 + 1) * P]), dma=True)
            A("vector", lambda e, cc=cc: e.tensor_copy(out=wfkb[cc], in_=wld4))
        for tb in range(NTB):
            for cc in range(NCH):
                A("tensor", lambda e, cc=cc, tb=tb: e.matmul(
                    ps[0][:], wfkb[cc], xk2_fm[cc][:, tb * 512:(tb + 1) * 512],
                    start=(cc == 0), stop=(cc == NCH - 1)))
            A("scalar", lambda e, tb=tb: e.activation(out=kfo[:, tb * 512:(tb + 1) * 512],
                                                      in_=ps[0][:], func=AF.Relu))
        A("scalar", lambda e: e.activation(out=kfqo, in_=kfo, func=AF.Square))
        A("sync", lambda e, o4=o4: e.dma_start(out=kfq_sc[o4 * P:(o4 + 1) * P, :], in_=kfqo), dma=True)

    cv = Carve(0)
    wfvb = [cv.bf(D) for _ in range(H_FFN // P)]
    wfrb = [cv.bf(D) for _ in range(NCH)]
    wld5 = cv.f32(D)
    kfl = cv.bf(P)
    xrl = cv.bf(P)
    kvt = cv.f32(D)
    rst = cv.f32(D)
    x25 = cv.f32(D)
    for ic in range(H_FFN // P):
        A("sync", lambda e, ic=ic: e.dma_start(out=wld5, in_=wfvT_in[ic * P:(ic + 1) * P, :]), dma=True)
        A("vector", lambda e, ic=ic: e.tensor_copy(out=wfvb[ic], in_=wld5))
    for cc in range(NCH):
        A("sync", lambda e, cc=cc: e.dma_start(out=wld5, in_=wfrT_in[cc * P:(cc + 1) * P, :]), dma=True)
        A("vector", lambda e, cc=cc: e.tensor_copy(out=wfrb[cc], in_=wld5))
    for i in range(NT):
        for half in range(2):
            for ic in range(H_FFN // P):
                A("sync", lambda e, ic=ic, i=i: e.dma_start(
                    out=kfl, in_=kfq_sc[ic * P:(ic + 1) * P, i * P:(i + 1) * P]), dma=True)
                A("tensor", lambda e, ic=ic, half=half: e.matmul(
                    ps[0][:], kfl, wfvb[ic][:, half * 512:(half + 1) * 512],
                    start=(ic == 0), stop=(ic == H_FFN // P - 1)))
            A("scalar", lambda e, half=half: e.activation(out=kvt[:, half * 512:(half + 1) * 512],
                                                          in_=ps[0][:], func=AF.Copy))
        for half in range(2):
            for cc in range(NCH):
                A("sync", lambda e, cc=cc, i=i: e.dma_start(
                    out=xrl, in_=xr2_sc[cc * P:(cc + 1) * P, i * P:(i + 1) * P]), dma=True)
                A("tensor", lambda e, cc=cc, half=half: e.matmul(
                    ps[0][:], xrl, wfrb[cc][:, half * 512:(half + 1) * 512],
                    start=(cc == 0), stop=(cc == NCH - 1)))
            A("scalar", lambda e, half=half: e.activation(out=rst[:, half * 512:(half + 1) * 512],
                                                          in_=ps[0][:], func=AF.Sigmoid))
        A("vector", lambda e: e.tensor_tensor(out=rst, in0=rst, in1=kvt, op=ALU.mult))
        A("sync", lambda e, i=i: e.dma_start(out=x25, in_=x2_sc[i * P:(i + 1) * P, :]), dma=True)
        A("vector", lambda e: e.tensor_tensor(out=rst, in0=rst, in1=x25, op=ALU.add))
        A("sync", lambda e, i=i: e.dma_start(out=attn1_o[i * P:(i + 1) * P, :], in_=rst), dma=True)
    sp.emit(es)
    es.close()
    return nc



_CACHED = {}


def _get(name, builder):
    if name not in _CACHED:
        _CACHED[name] = builder()
    return _CACHED[name]


def kernel(**inputs):
    ins = {k: np.asarray(v) for k, v in inputs.items()}
    C = np.ascontiguousarray
    cores = list(range(NCORES))

    def run(nc, maps):
        res = run_bass_kernel_spmd(nc, maps, core_ids=cores)
        return res.results

    ab = run(_get("AB", build_AB), [ab_in_maps(ins, c) for c in cores])

    def wq_maps(pref):
        wg = ins[f"{pref}_w_gate"].astype(np.float32)
        wu = ins[f"{pref}_w_up"].astype(np.float32)
        wd = ins[f"{pref}_w_down"].astype(np.float32)
        pad = np.zeros((3072 - I_FFN, D), np.float32)
        wgp = np.concatenate([wg, pad], 0)
        wup = np.concatenate([wu, pad], 0)
        return [{"wg_sh": C(wgp[c * 384:(c + 1) * 384]),
                 "wu_sh": C(wup[c * 384:(c + 1) * 384]),
                 "wd_sh": C(wd[c * P:(c + 1) * P])} for c in cores]

    def wq_assemble(res):
        wgqT = np.concatenate([res[c]["wgqT_sh"] for c in cores], 1)[:, :I_FFN]
        wuqT = np.concatenate([res[c]["wuqT_sh"] for c in cores], 1)[:, :I_FFN]
        wdqT = np.concatenate([res[c]["wdqT_sh"] for c in cores], 1)
        return C(wgqT), C(wuqT), C(wdqT)

    wq0 = run(_get("WQ", build_WQ), wq_maps("L0_ffn"))
    wgq0, wuq0, wdq0 = wq_assemble(wq0)

    amax = np.concatenate([ab[c]["pcmax"][0:1, 0:1] for c in cores], 1)
    eyes = np.concatenate([np.eye(P) * 7, np.eye(P) * 15, np.eye(P) * 13], 1).astype(np.float32)
    identb = np.eye(P).astype(BF)
    cres = run(_get("C", build_C), [{"k2": ab[c]["k2"], "amax": amax, "eyes": eyes,
                                     "identb": identb} for c in cores])

    identf = np.eye(P, dtype=np.float32)
    dres = run(_get("D", build_D), [{"q_fm": ab[c]["q_fm"], "k_fm": ab[c]["k_fm"],
                                     "v_tok": ab[c]["v_tok"], "qlm": ab[c]["qlm"],
                                     "klm": ab[c]["klm"], "zt": cres[c]["zt"],
                                     "identf": identf} for c in cores])

    f0 = run(_get("FFN", build_FFN), [{"xn_tok": ab[c]["xn_tok"], "wgq_fm": wgq0,
                                       "wuq_fm": wuq0, "wdqT": wdq0} for c in cores])

    epsc = np.full((P, 1), 1e-5, np.float32)
    x_full = ins["L0_attn_x"].astype(np.float32)
    fres = run(_get("F", build_F), [{"x": C(x_full[c * BC:(c + 1) * BC].reshape(T, D)),
                                     "attn_tok": dres[c]["attn_tok"],
                                     "ffn_tok": f0[c]["ffn_tok"], "epsc": epsc}
                                    for c in cores])

    wq1 = run(_get("WQ", build_WQ), wq_maps("L1_ffn"))
    wgq1, wuq1, wdq1 = wq_assemble(wq1)
    f1 = run(_get("FFN", build_FFN), [{"xn_tok": fres[c]["xn1_tok"], "wgq_fm": wgq1,
                                       "wuq_fm": wuq1, "wdqT": wdq1} for c in cores])

    bt = lambda v: C(np.broadcast_to(np.asarray(v, np.float32), (P, D)))
    gim_common = {
        "ln0w": bt(ins["L1_attn_ln0_w"]), "ln0b": bt(ins["L1_attn_ln0_b"]),
        "ln1w": bt(ins["L1_attn_ln1_w"]), "ln1b": bt(ins["L1_attn_ln1_b"]),
        "ln2w": bt(ins["L1_attn_ln2_w"]), "ln2b": bt(ins["L1_attn_ln2_b"]),
        "tmkb": bt(ins["L1_attn_tmk"]), "tmvb": bt(ins["L1_attn_tmv"]),
        "tmrb": bt(ins["L1_attn_tmr"]), "cmkb": bt(ins["L1_attn_cmk"]),
        "cmrb": bt(ins["L1_attn_cmr"]),
        "wkT": C(ins["L1_attn_Wk"].astype(np.float32).T),
        "wvT": C(ins["L1_attn_Wv"].astype(np.float32).T),
        "wrT": C(ins["L1_attn_Wr"].astype(np.float32).T),
        "woT": C(ins["L1_attn_Wo"].astype(np.float32).T),
        "wfkT": C(ins["L1_attn_Wfk"].astype(np.float32).T),
        "wfvT": C(ins["L1_attn_Wfv"].astype(np.float32).T),
        "wfrT": C(ins["L1_attn_Wfr"].astype(np.float32).T),
        "tdc": C(ins["L1_attn_time_decay"].astype(np.float32).reshape(NCH, P).T),
        "tfc": C(ins["L1_attn_time_first"].astype(np.float32).reshape(NCH, P).T),
        "epsc": epsc,
    }
    gres = run(_get("G", build_G), [dict(gim_common, xn1_tok=fres[c]["xn1_tok"])
                                    for c in cores])

    ires = run(_get("I", build_I), [{"x1_tok": fres[c]["x1_tok"],
                                     "attn1_tok": gres[c]["attn1_tok"],
                                     "ffn1_tok": f1[c]["ffn_tok"]} for c in cores])

    out = np.stack([ires[c]["out"] for c in cores]).reshape(16, S, D).astype(np.float32)
    return out


if __name__ == "__main__":
    print("library module")
